# version 08
# baseline (speedup 1.0000x reference)
import numpy as np
import ml_dtypes
import concourse.bass as bass
import concourse.mybir as mybir
from concourse.bass_utils import run_bass_kernel_spmd

F32 = mybir.dt.float32
BF16 = mybir.dt.bfloat16
AF = mybir.ActivationFunctionType
ALU = mybir.AluOpType
AX = mybir.AxisListType

D = 1024
H = 16
HD = 64
NTOK = 4096
C = 128
NCH = NTOK // C
KC = 8
SM = 216
EPS_LN = 1e-5


class Sched:

    def __init__(self):
        self.ops = []
        self.val = {}

    def add(self, eng, key, waits, fn, inc=1, fence=True):
        assert key is None or key not in self.val
        self.ops.append((eng, key, list(waits), fn, inc, fence))
        if key is not None:
            self.val[key] = None

    FENCED = ("vector", "scalar", "tensor")

    def finalize(self):
        ctr = {}
        for eng, key, waits, fn, inc, fence in self.ops:
            ctr[eng] = ctr.get(eng, 0) + inc
            if key is not None:
                self.val[key] = (eng, ctr[eng])
        for k, v in self.val.items():
            assert v is not None, k
        self.totals = ctr

    def emit(self, eng_name, eng, sems):
        last_wait = {}
        for oeng, key, waits, fn, inc, fence in self.ops:
            if oeng != eng_name:
                continue
            for wkey in waits:
                peng, pval = self.val[wkey]
                if peng == eng_name:
                    continue
                if last_wait.get(peng, -1) >= pval:
                    continue
                eng.wait_ge(sems[peng], pval)
                last_wait[peng] = pval
            instr = fn(eng)
            if instr == "selfinc":
                pass
            elif instr is not None:
                if fence and eng_name in self.FENCED:
                    instr = eng.drain(fusable=False)
                instr.then_inc(sems[eng_name], inc)
            else:
                assert key is None and inc == 0


def expand_ap(t_ap, outer_num, inner_rep, col_step=1, col_off=0):
    es = t_ap.ap[0][0]
    return bass.AP(t_ap.tensor, t_ap.offset + col_off,
                   [[es, 128], [col_step, outer_num], [0, inner_rep]])


def view3(t_ap, outer_num, outer_step, inner_num, inner_step=1, off=0, npart=128, poff=0):
    es = t_ap.ap[0][0]
    return bass.AP(t_ap.tensor, t_ap.offset + off + poff * es,
                   [[es, npart], [outer_step, outer_num], [inner_step, inner_num]])


def build_program(debug=False, nch=NCH, dumpch=0, nanlog=False):
    nc = bass.Bass("TRN2", target_bir_lowering=False)

    x_in = nc.dram_tensor("x", [NTOK, D], F32, kind="ExternalInput")
    y_out = nc.dram_tensor("y", [NTOK, D], F32, kind="ExternalOutput")
    wq_d = nc.dram_tensor("wq", [128, KC * D], F32, kind="ExternalInput")
    wk_d = nc.dram_tensor("wk", [128, KC * D], F32, kind="ExternalInput")
    wv_d = nc.dram_tensor("wv", [128, KC * D], F32, kind="ExternalInput")
    wo_d = nc.dram_tensor("wo", [128, KC * D], F32, kind="ExternalInput")
    wf_d = nc.dram_tensor("wf", [128, KC * D], F32, kind="ExternalInput")
    sm_d = nc.dram_tensor("sm", [128, KC * SM], F32, kind="ExternalInput")
    wg2_d = nc.dram_tensor("wg2", [64, D], F32, kind="ExternalInput")
    vf_d = nc.dram_tensor("vf", [128, D], F32, kind="ExternalInput")
    lnw_d = nc.dram_tensor("lnw", [128, KC], F32, kind="ExternalInput")
    cons_d = nc.dram_tensor("cons", [128, 1280], F32, kind="ExternalInput")

    from contextlib import ExitStack
    with ExitStack() as ctx:
        E = ctx.enter_context
        wq = E(nc.sbuf_tensor("wq_b", [128, KC * D], BF16))
        wk = E(nc.sbuf_tensor("wk_b", [128, KC * D], BF16))
        wv = E(nc.sbuf_tensor("wv_b", [128, KC * D], BF16))
        wo = E(nc.sbuf_tensor("wo_b", [128, KC * D], BF16))
        wf = E(nc.sbuf_tensor("wf_b", [128, KC * D], BF16))
        sm = E(nc.sbuf_tensor("sm_b", [128, KC * SM], BF16))
        wg2 = E(nc.sbuf_tensor("wg2_b", [64, D], BF16))
        vf = E(nc.sbuf_tensor("vf_b", [128, D], BF16))
        lnw = E(nc.sbuf_tensor("lnw_b", [128, KC], F32))
        stg = [E(nc.sbuf_tensor(f"stg{i}", [128, KC * D // 4], F32)) for i in range(2)]
        cons = E(nc.sbuf_tensor("cons_sb", [128, 1280], F32))
        id_bf = E(nc.sbuf_tensor("id_bf", [128, 128], BF16))

        xb = [E(nc.sbuf_tensor(f"x{i}", [128, D], F32)) for i in range(3)]
        ob = [E(nc.sbuf_tensor(f"o{i}", [128, D], F32)) for i in range(2)]
        xn_tm = E(nc.sbuf_tensor("xn_tm", [128, D], BF16))
        xn_fm = [E(nc.sbuf_tensor(f"xnf{i}", [128, D], BF16)) for i in range(2)]
        q_f = E(nc.sbuf_tensor("q_f", [128, D], F32))
        k_f = E(nc.sbuf_tensor("k_f", [128, D], F32))
        v_b = E(nc.sbuf_tensor("v_b", [128, D], BF16))
        qs_tm = E(nc.sbuf_tensor("qs_tm", [128, D], BF16))
        kns_tm = E(nc.sbuf_tensor("kns_tm", [128, D], BF16))
        kp_tm = E(nc.sbuf_tensor("kp_tm", [128, D], BF16))
        qs_fm = E(nc.sbuf_tensor("qs_fm", [128, D], BF16))
        kn_fm = E(nc.sbuf_tensor("kn_fm", [128, D], BF16))
        sqs = E(nc.sbuf_tensor("sqs", [128, D], F32))
        tg = E(nc.sbuf_tensor("tg", [128, D], F32))
        gate = E(nc.sbuf_tensor("gate", [128, D], F32))
        og = E(nc.sbuf_tensor("og", [128, D], F32))
        ogn_tm = E(nc.sbuf_tensor("ogn_tm", [128, D], BF16))
        ogn_fm = E(nc.sbuf_tensor("ogn_fm", [128, D], BF16))
        P_bf = E(nc.sbuf_tensor("P_bf", [128, H * 128], BF16))
        S = E(nc.sbuf_tensor("S", [64, 1024], F32))
        T0f = E(nc.sbuf_tensor("T0f", [64, 1024], F32))
        S_bf = E(nc.sbuf_tensor("S_bf", [128, 512], BF16))
        tmpS = E(nc.sbuf_tensor("tmpS", [64, 512], F32))
        qv_tm = E(nc.sbuf_tensor("qv_tm", [128, D], BF16))
        qv_fm = E(nc.sbuf_tensor("qv_fm", [128, D], BF16))
        E2 = E(nc.sbuf_tensor("E2", [128, 16], F32))
        g1_b = E(nc.sbuf_tensor("g1_b", [128, 64], BF16))
        g1_fm = E(nc.sbuf_tensor("g1_fm", [64, 128], BF16))
        t_f = [E(nc.sbuf_tensor(f"t_f{i}", [128, 128], F32)) for i in range(2)]
        lg = [E(nc.sbuf_tensor(f"lg{i}", [128, 8], F32)) for i in range(2)]
        nmx = [E(nc.sbuf_tensor(f"nmx{i}", [128, 1], F32)) for i in range(2)]
        tp_b = E(nc.sbuf_tensor("tp_b", [128, 128], BF16))
        tp_fm = E(nc.sbuf_tensor("tp_fm", [128, 128], BF16))
        fz = E(nc.sbuf_tensor("fz", [128, 16], F32))
        fu = E(nc.sbuf_tensor("fu", [128, 16], F32))
        sp = E(nc.sbuf_tensor("sp", [128, 16], F32))
        B3 = E(nc.sbuf_tensor("B3", [128, 112], F32))
        dk = E(nc.sbuf_tensor("dk", [128, 112], F32))
        sENG = E(nc.sbuf_tensor("sENG", [128, 32], F32))


        nrm = E(nc.sbuf_tensor("nrm", [128, 16], F32))
        nrm2 = E(nc.sbuf_tensor("nrm2", [128, 16], F32))
        invn = E(nc.sbuf_tensor("invn", [128, 16], F32))
        e8 = E(nc.sbuf_tensor("e8", [128, 8], F32))
        gfin = E(nc.sbuf_tensor("gfin", [128, 8], F32))
        se8 = E(nc.sbuf_tensor("se8", [128, 1], F32))
        rs8 = E(nc.sbuf_tensor("rs8", [128, 1], F32))
        sumx = E(nc.sbuf_tensor("sumx", [128, 1], F32))
        sumq = E(nc.sbuf_tensor("sumq", [128, 1], F32))
        mean = E(nc.sbuf_tensor("mean", [128, 1], F32))
        mean2 = E(nc.sbuf_tensor("mean2", [128, 1], F32))
        var = E(nc.sbuf_tensor("var", [128, 1], F32))
        lvar = E(nc.sbuf_tensor("lvar", [128, 1], F32))
        istd = E(nc.sbuf_tensor("istd", [128, 1], F32))
        sumx2 = E(nc.sbuf_tensor("sumx2", [128, 1], F32))
        sumq2 = E(nc.sbuf_tensor("sumq2", [128, 1], F32))
        meanB = E(nc.sbuf_tensor("meanB", [128, 1], F32))
        mean2B = E(nc.sbuf_tensor("mean2B", [128, 1], F32))
        varB = E(nc.sbuf_tensor("varB", [128, 1], F32))
        lvarB = E(nc.sbuf_tensor("lvarB", [128, 1], F32))
        istdB = E(nc.sbuf_tensor("istdB", [128, 1], F32))
        ogc = E(nc.sbuf_tensor("ogc", [128, 1024], F32))
        m2n = E(nc.sbuf_tensor("m2n", [128, 1], F32))
        varf = E(nc.sbuf_tensor("varf", [128, 1], F32))
        m2nB = E(nc.sbuf_tensor("m2nB", [128, 1], F32))
        varfB = E(nc.sbuf_tensor("varfB", [128, 1], F32))
        dbg1 = E(nc.sbuf_tensor("dbg1", [128, 1024], F32))
        dbg2 = E(nc.sbuf_tensor("dbg2", [128, 16], F32))
        dbg3 = E(nc.sbuf_tensor("dbg3", [128, 1], F32))
        dbg4 = E(nc.sbuf_tensor("dbg4", [128, 1], F32))

        ps_q = E(nc.psum_tensor("ps_q", [128, 1024], F32))
        ps_k = E(nc.psum_tensor("ps_k", [128, 1024], F32))
        ps_v = E(nc.psum_tensor("ps_v", [128, 1024], F32))
        ps_s = E(nc.psum_tensor("ps_s", [128, 512], F32))
        ps_tr = E(nc.psum_tensor("ps_tr", [128, 1024], BF16))

        sem_names = ["sync", "gpsimd", "tensor", "vector", "scalar"]
        sems = {n: E(nc.semaphore(f"s_{n}")) for n in sem_names}
        block = E(nc.Block())

        nc.const_aps.register(F32, EPS_LN, cons[:, 1100:1101])
        nc.const_aps.register(F32, 1.0, cons[:, 1101:1102])
        cd1 = cons[:, 0:128]
        cd1n = cons[:, 128:256]
        cd3 = cons[:, 256:384]
        cd4 = cons[:, 384:512]
        cd5 = cons[:, 512:640]
        ctri = cons[:, 640:768]
        cn31 = cons[:, 768:896]
        ident_f = cons[:, 896:1024]
        mask64 = cons[:, 1024:1088]

        sch = Sched()

        def dma(dst, src):
            return lambda e: e.dma_start(out=dst, in_=src)

        sch.add("sync", ("ld_cons", 0), [], dma(cons[:], cons_d[:]), 16)

        def warmup(e):
            one = cons[:, 897:898]
            e.activation(lvar[:], one, AF.Silu)
            e.activation(lvar[:], one, AF.Tanh)
            e.activation(lvar[:], one, AF.Ln)
            e.activation(lvar[:], one, AF.Exp)
            return e.drain(fusable=False)
        sch.add("scalar", ("warm", 0), [("ld_cons", 0)], warmup)

        sch.add("sync", ("ld_x", 0), [("warm", 0)], dma(xb[0][:], x_in[0:128, :]), 16)
        QB = KC * D // 4
        sch.add("sync", ("ld_sm", 0), [], dma(stg[0][:, 0:KC * SM], sm_d[:]), 16)
        sch.add("vector", ("cast_sm", 0), [("ld_sm", 0)],
                lambda e: e.tensor_copy(sm[:], stg[0][:, 0:KC * SM]))
        sch.add("vector", ("cast_id", 0), [("ld_cons", 0)],
                lambda e: e.tensor_copy(id_bf[:], ident_f))
        big = [("wq", wq_d, wq), ("wk", wk_d, wk), ("wv", wv_d, wv)]
        prev_cast = {0: ("cast_sm", 0), 1: ("cast_sm", 0)}
        for nm, dsrc, dst in big:
            for hh in range(4):
                bix = hh % 2
                sch.add("sync", ("ld_" + nm, hh), [prev_cast[bix]],
                        dma(stg[bix][:], dsrc[:, hh * QB:(hh + 1) * QB]), 16)
                sch.add("vector", ("cast_" + nm, hh), [("ld_" + nm, hh)],
                        lambda e, dst=dst, hh=hh, bix=bix: e.tensor_copy(
                            dst[:, hh * QB:(hh + 1) * QB], stg[bix][:]))
                prev_cast[bix] = ("cast_" + nm, hh)
        sch.add("sync", ("ld_x", 1), [], dma(xb[1][:], x_in[128:256, :]), 16)
        sch.add("sync", ("ld_wg2", 0), [prev_cast[0]], dma(stg[0][0:64, 0:D], wg2_d[:]), 16)
        sch.add("vector", ("cast_wg2", 0), [("ld_wg2", 0)],
                lambda e: e.tensor_copy(wg2[:], stg[0][0:64, 0:D]))
        prev_cast[0] = ("cast_wg2", 0)
        for hh in range(4):
            bix = hh % 2
            sch.add("sync", ("ld_wf", hh), [prev_cast[bix]],
                    dma(stg[bix][:], wf_d[:, hh * QB:(hh + 1) * QB]), 16)
            sch.add("vector", ("cast_wf", hh), [("ld_wf", hh)],
                    lambda e, hh=hh, bix=bix: e.tensor_copy(wf[:, hh * QB:(hh + 1) * QB], stg[bix][:]))
            prev_cast[bix] = ("cast_wf", hh)
        sch.add("sync", ("ld_vf", 0), [prev_cast[0]], dma(stg[0][:, 0:D], vf_d[:]), 16)
        sch.add("vector", ("cast_vf", 0), [("ld_vf", 0)],
                lambda e: e.tensor_copy(vf[:], stg[0][:, 0:D]))
        prev_cast[0] = ("cast_vf", 0)
        sch.add("sync", ("ld_lnw", 0), [], dma(lnw[:], lnw_d[:]), 16)
        for hh in range(4):
            bix = hh % 2
            sch.add("sync", ("ld_wo", hh), [prev_cast[bix]],
                    dma(stg[bix][:], wo_d[:, hh * QB:(hh + 1) * QB]), 16)
            def cast_wo(e, hh=hh, bix=bix):
                r = None
                for c in range(KC // 4):
                    cc = hh * (KC // 4) + c
                    r = e.tensor_scalar(wo[:, cc * D:(cc + 1) * D],
                                        stg[bix][:, c * D:(c + 1) * D],
                                        lnw[:, cc:cc + 1], None, ALU.mult)
                return r
            sch.add("vector", ("cast_wo", hh), [("ld_wo", hh), ("ld_lnw", 0)], cast_wo)
            prev_cast[bix] = ("cast_wo", hh)
        sch.add("vector", ("zeroS", 0), [], lambda e: e.memset(S[:], 0.0))
        sch.add("vector", ("zeroP", 0), [], lambda e: e.memset(P_bf[:], 0.0))
        def zpsum(e):
            e.memset(ps_q[:], 0.0)
            e.memset(ps_k[:], 0.0)
            e.memset(ps_v[:], 0.0)
            e.memset(ps_s[:], 0.0)
            return e.drain(fusable=False)
        sch.add("vector", ("zpsum", 0), [], zpsum)
        if nanlog:
            stat = E(nc.sbuf_tensor("stat", [128, 9 * nch + 12], F32))
            stat_d = nc.dram_tensor("statd", [128, 9 * nch + 12], F32, kind="ExternalOutput")
            sch.add("vector", ("zstat", 0), [], lambda e: e.memset(stat[:], 0.0))
            def NL(j, n2, ms, buf, red=True):
                npp = buf.shape[0]
                col = stat[0:npp, 9 * n2 + j:9 * n2 + j + 1]
                if red:
                    fn = lambda e, col=col, buf=buf: e.reduce_sum(col, buf, AX.X)
                else:
                    fn = lambda e, col=col, buf=buf: e.tensor_copy(col, buf)
                sch.add("vector", ("nl%d" % j, n2), [ms], fn)

        def ztr(e):
            for k in range(8):
                r = e.transpose(ps_tr[:, k * 128:(k + 1) * 128], P_bf[:, 0:128], id_bf[:])
            return r
        sch.add("tensor", ("ztr", 0), [("zeroP", 0), ("cast_id", 0)], ztr)

        W_P1 = [("cast_wq", 3), ("cast_wk", 3), ("cast_wv", 3)]
        W_TAIL = [("cast_wf", 3), ("cast_vf", 0), ("cast_wo", 3)]

        he = [0, 2, 4, 6]

        def emit_H1(n):
            nb = n % 3
            def sumx_op(e, nb=nb):
                e.reduce_sum(sumx[:], xb[nb][:], AX.X)
                return e.drain(fusable=False)
            waits = [("ld_x", n)]
            if n == 0:
                waits.append(("cast_wo", 3))
            sch.add("vector", ("sumx", n), waits, sumx_op)
            def ln1_stats(e):
                e.tensor_scalar(mean[:], sumx[:], 1.0 / D, None, ALU.mult)
                return e.tensor_scalar(var[:], sumq[:], 1.0 / D, 1e-5, ALU.mult, ALU.add)
            sch.add("vector", ("ln1v", n), [("sumq", n)], ln1_stats)
            def ln1_istd(e):
                e.activation(mean2[:], mean[:], AF.Square)
                e.activation(lvar[:], mean2[:], AF.Ln, scale=-1.0, bias=var[:])
                return e.activation(istd[:], lvar[:], AF.Exp, scale=-0.5)
            sch.add("scalar", ("istd1", n), [("ln1v", n)], ln1_istd)
            sch.add("vector", ("xn", n), [("istd1", n)],
                    lambda e, nb=nb: e.tensor_scalar(xn_tm[:], xb[nb][:], mean[:],
                                                     istd[:], ALU.subtract, ALU.mult))
            def xn_T(e):
                r = None
                for c in range(KC):
                    r = e.transpose(ps_tr[:, c * 128:(c + 1) * 128],
                                    xn_tm[:, c * 128:(c + 1) * 128], id_bf[:])
                return r
            waits = [("xn", n), ("cast_id", 0)]
            waits.append(("ev_knT", n - 1) if n >= 1 else ("ztr", 0))
            sch.add("tensor", ("xn_T", n), waits, xn_T)
            sch.add("vector", ("ev_xn", n), [("xn_T", n)],
                    lambda e, nb2=(n % 2): e.tensor_copy(xn_fm[nb2][:], ps_tr[:]))
            if nanlog:
                NL(0, n, ("ev_xn", n), xn_fm[n % 2][:])

        def emit_sumq(n):
            nb = n % 3
            def sumq_op(e, nb=nb):
                e.memzero(sumq[:])
                return e.activation(sqs[:], xb[nb][:], AF.Square, accum_out=sumq[:])
            waits = [("ld_x", n), ("ld_cons", 0)]
            if n == 0:
                waits.append(("cast_wo", 3))
            sch.add("scalar", ("sumq", n), waits, sumq_op)

        emit_sumq(0)
        emit_H1(0)
        if nanlog:
            def NP(k, ms, buf, red=True):
                npp = buf.shape[0]
                col = stat[0:npp, 9 * nch + k:9 * nch + k + 1]
                if red:
                    fn = lambda e, col=col, buf=buf: e.reduce_sum(col, buf, AX.X)
                else:
                    fn = lambda e, col=col, buf=buf: e.tensor_copy(col, buf)
                sch.add("vector", ("np%d" % k, 0), [ms], fn)
            NP(0, ("ld_x", 0), xb[0][:])
            NP(1, ("sumx", 0), sumx[:], red=False)
            NP(2, ("sumq", 0), sumq[:], red=False)
            NP(3, ("ln1v", 0), mean[:], red=False)
            NP(4, ("ln1v", 0), var[:], red=False)
            NP(5, ("istd1", 0), mean2[:], red=False)
            NP(6, ("istd1", 0), lvar[:], red=False)
            NP(7, ("istd1", 0), istd[:], red=False)
            NP(8, ("istd1", 0), lvar[:], red=False)
            NP(9, ("istd1", 0), istd[:], red=False)
            NP(10, ("xn", 0), xn_tm[:])
            NP(11, ("ev_xn", 0), ps_tr[:])

        for n in range(NCH_UNROLL := nch + 1):
            tail = n - 1
            nb2 = n % 2
            pb2 = (n - 1) % 2

            if n < nch:
                def smalls_mm(e, nb2=nb2):
                    r = None
                    for c in range(KC):
                        r = e.matmul(ps_s[:, 0:SM], xn_fm[nb2][:, c * 128:(c + 1) * 128],
                                     sm[:, c * SM:(c + 1) * SM],
                                     start=(c == 0), stop=(c == KC - 1))
                    return r
                waits = [("ev_xn", n), ("cast_sm", 0)]
                waits.append(("SupdB", n - 1) if n >= 1 else ("zpsum", 0))
                sch.add("tensor", ("sm_mm", n), waits, smalls_mm)

                def smalls_ev(e, n=n):
                    e.tensor_copy(fz[:], ps_s[:, 0:16])
                    e.tensor_copy(lg[n % 2][:], ps_s[:, 16:24])
                    e.tensor_copy(g1_b[:], ps_s[:, 24:88])
                    e.tensor_copy(t_f[n % 2][:], ps_s[:, 88:216])
                    e.drain(fusable=False)
                    e.reduce_max(se8[:], lg[n % 2][:], AX.X)
                    e.drain(fusable=False)
                    return e.tensor_scalar(nmx[n % 2][:], se8[:], -1.0, None, ALU.mult)
                sch.add("vector", ("ev_sm", n), [("sm_mm", n)], smalls_ev)
                if nanlog:
                    NL(1, n, ("ev_sm", n), t_f[n % 2][:])

                def pass1(e, nb2=nb2):
                    r = None
                    for c in range(KC):
                        lhs = xn_fm[nb2][:, c * 128:(c + 1) * 128]
                        for (pst, wt) in ((ps_q, wq), (ps_k, wk), (ps_v, wv)):
                            for h in range(2):
                                r = e.matmul(pst[:, h * 512:(h + 1) * 512], lhs,
                                             wt[:, c * D + h * 512:c * D + (h + 1) * 512],
                                             start=(c == 0), stop=(c == KC - 1))
                    return r
                waits = [("ev_xn", n)] + W_P1
                if n >= 1:
                    waits += [("ev_og", n - 1), ("maskw", n - 1, 1), ("tgh", n - 1)]
                else:
                    waits.append(("zpsum", 0))
                sch.add("tensor", ("p1", n), waits, pass1)

                sch.add("tensor", ("g1T", n), [("ev_sm", n), ("ev_xn", n)],
                        lambda e: e.transpose(ps_tr[0:64, 0:128], g1_b[:], id_bf[:]))
                sch.add("vector", ("ev_g1", n), [("g1T", n)],
                        lambda e: e.tensor_copy(g1_fm[:], ps_tr[0:64, 0:128]))

                def fexp(e):
                    e.activation(fu[:], fz[:], AF.Exp, scale=-1.0)
                    return e.activation(sp[:], fu[:], AF.Ln, bias=1.0)
                sch.add("scalar", ("fsp", n), [("ev_sm", n)], fexp)
                if nanlog:
                    NL(2, n, ("fsp", n), sp[:])

                def B_mm(e):
                    for i, cm in enumerate([cd1, cd1n, cd3, cd4, cd5, ctri, cn31]):
                        e.matmul(ps_s[:, 16 * i:16 * (i + 1)], cm, sp[:],
                                 start=(i == 0), stop=(i == 6))
                    return e.matmul(ps_s[:, 96:112], cn31, sp[:], start=False, stop=True)
                sch.add("tensor", ("B_mm", n), [("fsp", n), ("ld_cons", 0)], B_mm)

                sch.add("vector", ("dsub", n), [("B_mm", n)],
                        lambda e: e.tensor_copy(B3[:], ps_s[:, 0:112]))
                sch.add("scalar", ("dexp", n), [("dsub", n)],
                        lambda e: e.activation(dk[:], B3[:, 0:112], AF.Exp))
                if nanlog:
                    NL(3, n, ("dexp", n), dk[:])

            if tail >= 0:
                tb2 = tail % 2
                tb3 = tail % 3
                def istd2(e):
                    e.activation(lvarB[:], varB[:], AF.Ln, bias=EPS_LN)
                    return e.activation(istdB[:], lvarB[:], AF.Exp, scale=-0.5)
                sch.add("scalar", ("istd2", tail), [("ln2v", tail)], istd2)
                if nanlog:
                    NL(7, tail, ("istd2", tail), istdB[:], red=False)
                sch.add("scalar", ("smax", tail), [("ev_sm", tail)],
                        lambda e, tb2=tb2: e.activation(e8[:], lg[tb2][:], AF.Exp, bias=nmx[tb2][:]))
                def smx2(e, tb2=tb2):
                    e.reduce_sum(se8[:], e8[:], AX.X)
                    e.drain(fusable=False)
                    e.reciprocal(rs8[:], se8[:])
                    e.drain(fusable=False)
                    e.tensor_scalar(gfin[:], e8[:], rs8[:], 1.0 / 16.0, ALU.mult, ALU.mult)
                    e.drain(fusable=False)
                    return e.tensor_tensor(tp_b[:], t_f[tb2][:], expand_ap(gfin[:], 8, 16), ALU.mult)
                sch.add("vector", ("tp", tail), [("smax", tail)], smx2)
                sch.add("vector", ("ogn", tail), [("istd2", tail)],
                        lambda e: e.tensor_scalar(ogn_tm[:], ogc[:], istdB[:], None, ALU.mult))
                def ogn_T(e):
                    r = None
                    for c in range(KC):
                        r = e.transpose(ps_tr[:, c * 128:(c + 1) * 128],
                                        ogn_tm[:, c * 128:(c + 1) * 128], id_bf[:])
                    return r
                waits = [("ogn", tail)]
                if n < nch:
                    waits.append(("ev_g1", n))
                else:
                    waits.append(("ev_knT", tail))
                sch.add("tensor", ("ogn_T", tail), waits, ogn_T)
                sch.add("vector", ("ev_ogn", tail), [("ogn_T", tail)],
                        lambda e: e.tensor_copy(ogn_fm[:], ps_tr[:]))
                sch.add("tensor", ("tp_T", tail), [("tp", tail), ("ev_ogn", tail)],
                        lambda e: e.transpose(ps_tr[:, 0:128], tp_b[:], id_bf[:]))
                sch.add("vector", ("ev_tp", tail), [("tp_T", tail)],
                        lambda e: e.tensor_copy(tp_fm[:], ps_tr[:, 0:128]))

                def tail_mm(e, tb2=tb2):
                    r = None
                    for c in range(KC):
                        lhs = xn_fm[tb2][:, c * 128:(c + 1) * 128]
                        for h in range(2):
                            r = e.matmul(ps_v[:, h * 512:(h + 1) * 512], lhs,
                                         wf[:, c * D + h * 512:c * D + (h + 1) * 512],
                                         start=(c == 0), stop=False)
                    for h in range(2):
                        r = e.matmul(ps_v[:, h * 512:(h + 1) * 512], tp_fm[:],
                                     vf[:, h * 512:(h + 1) * 512], start=False, stop=False)
                    for c in range(KC):
                        lhs = ogn_fm[:, c * 128:(c + 1) * 128]
                        for h in range(2):
                            r = e.matmul(ps_v[:, h * 512:(h + 1) * 512], lhs,
                                         wo[:, c * D + h * 512:c * D + (h + 1) * 512],
                                         start=False, stop=(c == KC - 1))
                    return r
                waits = [("ev_tp", tail), ("ev_ogn", tail)] + W_TAIL
                if n < nch:
                    waits.append(("si_v", n))
                else:
                    waits.append(("tgh", tail))
                sch.add("tensor", ("tmm", tail), waits, tail_mm)

                def fadd(e, tail=tail, tb3=tb3):
                    e.tensor_add(ob[tail % 2][:, 0:512], ps_v[:, 0:512], xb[tb3][:, 0:512])
                    return e.tensor_add(ob[tail % 2][:, 512:1024], ps_v[:, 512:1024],
                                        xb[tb3][:, 512:1024])
                fwaits = [("tmm", tail)]
                if tail >= 2:
                    fwaits.append(("st", tail - 2))
                sch.add("vector", ("fadd", tail), fwaits, fadd)
                if nanlog:
                    NL(8, tail, ("fadd", tail), ob[tail % 2][:])
                sch.add("sync", ("st", tail), [("fadd", tail)],
                        lambda e, tail=tail: e.dma_start(
                            out=y_out[tail * 128:(tail + 1) * 128, :],
                            in_=ob[tail % 2][:]), 16)
                if n + 2 < nch:
                    sch.add("sync", ("ld_x", n + 2), [("fadd", tail)],
                            lambda e, n=n: e.dma_start(
                                out=xb[(n + 2) % 3][:],
                                in_=x_in[(n + 2) * 128:(n + 3) * 128, :]), 16)
            elif n + 2 < nch:
                sch.add("sync", ("ld_x", n + 2), [],
                        lambda e, n=n: e.dma_start(
                            out=xb[(n + 2) % 3][:],
                            in_=x_in[(n + 2) * 128:(n + 3) * 128, :]), 16)

            if n < nch:
                waits = [("p1", n)]
                if n == 0:
                    waits.append(("cast_wo", 3))
                sch.add("scalar", ("si_v", n), waits,
                        lambda e: e.activation(v_b[:], ps_v[:], AF.Silu))
                def si_qk(e):
                    e.activation(q_f[:], ps_q[:], AF.Silu)
                    e.activation(k_f[:], ps_k[:], AF.Silu)
                    return e.activation(sqs[:], k_f[:], AF.Square)
                sch.add("scalar", ("si_qk", n), [("p1", n)], si_qk)

                def nred_op(e):
                    e.tensor_reduce(nrm[:], sqs[:].rearrange("p (h w) -> p h w", h=16),
                                    AX.X, ALU.add)
                    e.drain(fusable=False)
                    return e.tensor_scalar_max(nrm2[:], nrm[:], 1e-24)
                sch.add("vector", ("nmax", n), [("si_qk", n)], nred_op)

                def wg2_mm(e):
                    e.matmul(ps_v[:, 0:512], g1_fm[:], wg2[:, 0:512], start=True, stop=True)
                    return e.matmul(ps_v[:, 512:1024], g1_fm[:], wg2[:, 512:1024],
                                    start=True, stop=True)
                waits = [("ev_g1", n), ("cast_wg2", 0)]
                waits.append(("fadd", n - 1) if n >= 1 else ("si_v", n))
                sch.add("tensor", ("wg2", n), waits, wg2_mm)
                sch.add("scalar", ("tgh", n), [("wg2", n)],
                        lambda e: e.activation(tg[:], ps_v[:], AF.Tanh, scale=0.5))
                sch.add("vector", ("gate", n), [("tgh", n)],
                        lambda e: e.tensor_scalar(gate[:], tg[:], 0.5, 0.5, ALU.mult, ALU.add))

                def rsq(e):
                    e.activation(nrm[:], nrm2[:], AF.Ln)
                    return e.activation(invn[:], nrm[:], AF.Exp, scale=-0.5)
                sch.add("scalar", ("rsq", n), [("nmax", n)], rsq)
                def scl(e):
                    iv2 = bass.AP(invn[:].tensor, invn[:].offset,
                                  [[16, 128], [0, 2], [1, 16]])
                    return e.tensor_tensor(sENG[:], dk[:, 16:48], iv2, ALU.mult)
                sch.add("vector", ("scl", n), [("rsq", n), ("dexp", n)], scl)
                if nanlog:
                    NL(4, n, ("scl", n), sENG[:])

                sch.add("vector", ("qs", n), [("si_qk", n), ("dexp", n)],
                        lambda e: e.tensor_tensor(qs_tm[:], q_f[:], expand_ap(dk[:, 0:16], 16, 64), ALU.mult))
                sch.add("vector", ("qv", n), [("si_qk", n), ("dexp", n)],
                        lambda e: e.tensor_tensor(qv_tm[:], q_f[:], expand_ap(dk[:, 80:96], 16, 64), ALU.mult))
                sch.add("vector", ("kns", n), [("scl", n)],
                        lambda e: e.tensor_tensor(kns_tm[:], k_f[:], expand_ap(sENG[:, 0:16], 16, 64), ALU.mult))
                sch.add("vector", ("kp", n), [("scl", n)],
                        lambda e: e.tensor_tensor(kp_tm[:], k_f[:], expand_ap(sENG[:, 16:32], 16, 64), ALU.mult))

                def T0(e, n=n):
                    e.tensor_tensor(T0f[:], S[:].rearrange("p (h v) -> p h v", h=16),
                                    view3(dk[:, 96:112], 16, 1, 64, 0, npart=64), ALU.mult)
                    e.tensor_copy(S_bf[0:64, :].rearrange("p (h v) -> p h v", h=8),
                                  view3(S[:], 8, 128, 64, 1, npart=64))
                    e.tensor_copy(S_bf[64:128, :].rearrange("p (h v) -> p h v", h=8),
                                  view3(S[:], 8, 128, 64, 1, off=64, npart=64))
                    return e.drain(fusable=False)
                waits = [("dexp", n)]
                if n >= 1:
                    waits.append(("SupdB", n - 1))
                sch.add("vector", ("T0", n), waits, T0)

                def mkT(src):
                    def f(e):
                        r = None
                        for c in range(KC):
                            r = e.transpose(ps_tr[:, c * 128:(c + 1) * 128],
                                            src[:, c * 128:(c + 1) * 128], id_bf[:])
                        return r
                    return f
                waits = [("qs", n)]
                waits.append(("ev_tp", tail) if tail >= 0 else ("ev_g1", n))
                sch.add("tensor", ("qs_T", n), waits, mkT(qs_tm))
                sch.add("vector", ("ev_qsT", n), [("qs_T", n)],
                        lambda e: e.tensor_copy(qs_fm[:], ps_tr[:]))
                sch.add("tensor", ("qv_T", n), [("qv", n), ("ev_qsT", n)], mkT(qv_tm))
                sch.add("vector", ("ev_qvT", n), [("qv_T", n)],
                        lambda e: e.tensor_copy(qv_fm[:], ps_tr[:]))
                sch.add("tensor", ("kn_T", n), [("kns", n), ("ev_qvT", n)], mkT(kns_tm))
                sch.add("vector", ("ev_knT", n), [("kn_T", n)],
                        lambda e: e.tensor_copy(kn_fm[:], ps_tr[:]))

                if n + 1 < nch:
                    emit_sumq(n + 1)
                    emit_H1(n + 1)

                def scores(e, w, n=n):
                    r = None
                    first = {0: True, 1: True}
                    for i, ho in enumerate(he):
                        for par in range(2):
                            h = 8 * w + ho + par
                            lhs = kn_fm[64 * par:64 * par + 64,
                                        128 * (h // 2):128 * (h // 2) + 128]
                            rhs = qs_fm[64 * par:64 * par + 64,
                                        128 * (h // 2):128 * (h // 2) + 128]
                            out = ps_q[:, 512 * par + 128 * i:512 * par + 128 * (i + 1)]
                            r = e.matmul(out, lhs, rhs, start=first[par], stop=True,
                                         tile_position=(64 * par, 0))
                            first[par] = False
                    return r

                def maskw(e, w, n=n):
                    r = None
                    for par in range(2):
                        src00 = view3(ps_q[:], 4, 128, 64, 1, off=512 * par, npart=64)
                        dst00 = view3(P_bf[:], 4, 256, 64, 1,
                                      off=1024 * w + 128 * par, npart=64)
                        m64 = view3(mask64, 4, 0, 64, 1, npart=64)
                        e.tensor_tensor(dst00, src00, m64, ALU.mult)
                        src11 = view3(ps_q[:], 4, 128, 64, 1, off=512 * par + 64,
                                      npart=64, poff=64)
                        dst11 = view3(P_bf[:], 4, 256, 64, 1,
                                      off=1024 * w + 128 * par + 64, npart=64, poff=64)
                        m64b = view3(mask64, 4, 0, 64, 1, npart=64, poff=64)
                        e.tensor_tensor(dst11, src11, m64b, ALU.mult)
                        srcX = view3(ps_q[:], 4, 128, 64, 1, off=512 * par + 64, npart=64)
                        dstX = view3(P_bf[:], 4, 256, 64, 1,
                                     off=1024 * w + 128 * par + 64, npart=64)
                        corrA = view3(dk[:, 64:80], 4, 2, 64, 0, off=8 * w + par, npart=64)
                        r = e.tensor_tensor(dstX, srcX, corrA, ALU.mult)
                    return r

                def omm(e, w, n=n):
                    r = None
                    for i, ho in enumerate(he):
                        for par in range(2):
                            h = 8 * w + ho + par
                            lhs = qv_fm[64 * par:64 * par + 64,
                                        128 * (h // 2):128 * (h // 2) + 128]
                            rhs = S_bf[64 * par:64 * par + 64,
                                       64 * (h // 2):64 * (h // 2) + 64]
                            out = ps_k[:, 512 * par + 64 * (h // 2):512 * par + 64 * (h // 2) + 64]
                            r = e.matmul(out, lhs, rhs,
                                         start=(w == 0 and i == 0), stop=False,
                                         tile_position=(64 * par, 0))
                    for i, ho in enumerate(he):
                        for par in range(2):
                            h = 8 * w + ho + par
                            lhs = P_bf[:, 128 * h:128 * (h + 1)]
                            rhs = v_b[:, 64 * h:64 * (h + 1)]
                            out = ps_k[:, 512 * par + 64 * (h // 2):512 * par + 64 * (h // 2) + 64]
                            r = e.matmul(out, lhs, rhs, start=False, stop=True)
                    return r

                def kvmm(e, rr, n=n):
                    r = None
                    for hh in range(8):
                        h = 8 * rr + hh
                        out = ps_s[0:64, 64 * hh:64 * hh + 64]
                        r = e.matmul(out, kp_tm[:, 64 * h:64 * (h + 1)],
                                     v_b[:, 64 * h:64 * (h + 1)],
                                     start=(hh == 0), stop=True)
                    return r

                def Supd(e, rr, n=n):
                    e.tensor_tensor(tmpS[:].rearrange("p (h v) -> p h v", h=8),
                                    T0f[:, 512 * rr:512 * (rr + 1)].rearrange("p (h v) -> p h v", h=8),
                                    view3(dk[:, 48:64], 8, 1, 64, 0, off=8 * rr, npart=64), ALU.mult)
                    e.drain(fusable=False)
                    return e.tensor_add(S[:, 512 * rr:512 * (rr + 1)], tmpS[:],
                                        ps_s[0:64, 0:512])

                sch.add("tensor", ("scores", n, 0), [("ev_knT", n), ("si_qk", n)],
                        lambda e: scores(e, 0))
                sch.add("tensor", ("kv", n, 0), [("kp", n), ("dsub", n), ("dexp", n)],
                        lambda e: kvmm(e, 0))
                sch.add("vector", ("maskw", n, 0), [("scores", n, 0)],
                        lambda e: maskw(e, 0))
                sch.add("tensor", ("omm", n, 0),
                        [("maskw", n, 0), ("T0", n), ("ev_qvT", n), ("si_qk", n)],
                        lambda e: omm(e, 0))
                sch.add("vector", ("SupdA", n), [("kv", n, 0), ("T0", n)],
                        lambda e: Supd(e, 0))
                sch.add("tensor", ("scores", n, 1), [("maskw", n, 0)],
                        lambda e: scores(e, 1))
                sch.add("tensor", ("kv", n, 1), [("SupdA", n)],
                        lambda e: kvmm(e, 1))
                sch.add("vector", ("maskw", n, 1), [("scores", n, 1)],
                        lambda e: maskw(e, 1))
                sch.add("tensor", ("omm", n, 1), [("maskw", n, 1)],
                        lambda e: omm(e, 1))
                sch.add("vector", ("SupdB", n), [("kv", n, 1)],
                        lambda e: Supd(e, 1))
                if nanlog:
                    NL(5, n, ("SupdB", n), S[:])

                def ogev(e, n=n):
                    r = None
                    for par in range(2):
                        src = view3(ps_k[:], 8, 64, 64, 1, off=512 * par)
                        gA = view3(gate[:], 8, 128, 64, 1, off=64 * par)
                        dst = view3(og[:], 8, 128, 64, 1, off=64 * par)
                        r = e.tensor_tensor(dst, src, gA, ALU.mult)
                    return r
                sch.add("vector", ("ev_og", n), [("omm", n, 1), ("gate", n)], ogev)

                def sumx2_op(e):
                    e.reduce_sum(sumx2[:], og[:], AX.X)
                    e.drain(fusable=False)
                    e.tensor_scalar(meanB[:], sumx2[:], 1.0 / D, None, ALU.mult)
                    e.drain(fusable=False)
                    return e.tensor_scalar(ogc[:], og[:], meanB[:], None, ALU.subtract)
                sch.add("vector", ("sumx2", n), [("ev_og", n)], sumx2_op)
                def sqog_op(e):
                    e.memzero(sumq2[:])
                    return e.activation(sqs[:], ogc[:], AF.Square, accum_out=sumq2[:])
                sch.add("scalar", ("sq_og", n), [("sumx2", n)], sqog_op)
                sch.add("vector", ("ln2v", n), [("sq_og", n)],
                        lambda e: e.tensor_scalar(varB[:], sumq2[:], 1.0 / D, None, ALU.mult))
                if nanlog:
                    NL(6, n, ("ln2v", n), varB[:], red=False)

        if nanlog:
            sch.add("sync", ("st_stat", 0), [("nl8", nch - 1), ("nl7", nch - 1)],
                    lambda e: e.dma_start(out=stat_d[:], in_=stat[:]), 16)

        if debug:
            for cn in range(nch):
                for nm2, t2, dt2, ms2 in [("cB3", B3, F32, ("dsub", cn)),
                                          ("cinv", invn, F32, ("rsq", cn)),
                                          ("cS", S, F32, ("SupdB", cn)),
                                          ("cog", og, F32, ("ev_og", cn)),
                                          ("cfz", fz, F32, ("ev_sm", cn)),
                                          ("csp", sp, F32, ("fsp", cn)),
                                          ("ckp", kp_tm, BF16, ("kp", cn)),
                                          ("ckns", kns_tm, BF16, ("kns", cn)),
                                          ("cqs", qs_tm, BF16, ("qs", cn)),
                                          ("cqv", qv_tm, BF16, ("qv", cn)),
                                          ("cvb", v_b, BF16, ("si_v", cn)),
                                          ("cqf", q_f, F32, ("si_qk", cn)),
                                          ("ckf", k_f, F32, ("si_qk", cn)),
                                          ("cqsf", qs_fm, BF16, ("ev_qsT", cn)),
                                          ("cknf", kn_fm, BF16, ("ev_knT", cn)),
                                          ("cqvf", qv_fm, BF16, ("ev_qvT", cn)),
                                          ("cP", P_bf, BF16, ("maskw", cn, 1)),
                                          ("cSb", S_bf, BF16, ("T0", cn)),
                                          ("cT0", T0f, F32, ("T0", cn)),
                                          ("cgate", gate, F32, ("gate", cn)),
                                          ("csE", sENG, F32, ("scl", cn))]:
                    shp2 = [t2[:].shape[0], t2[:].shape[1]]
                    dd2 = nc.dram_tensor(f"{nm2}_{cn}", shp2, dt2, kind="ExternalOutput")
                    sch.add("gpsimd", ("dbgc", nm2, cn), [ms2],
                            lambda e, dd2=dd2, t2=t2: e.dma_start(out=dd2[:], in_=t2[:]), 16)

            for wnm, wt2 in [("d_wq", wq), ("d_wk", wk), ("d_wv", wv), ("d_wo", wo), ("d_wf", wf)]:
                dd3 = nc.dram_tensor(wnm, [128, KC * D], BF16, kind="ExternalOutput")
                sch.add("gpsimd", ("dbgw", wnm), [("cast_wo", 3)],
                        lambda e, dd3=dd3, wt2=wt2: e.dma_start(out=dd3[:], in_=wt2[:]), 16)
            DBG = [
                ("d_xn_tm", xn_tm, BF16, ("xn", dumpch),),
                ("d_xn_fm", xn_fm[0], BF16, ("ev_xn", dumpch),),
                ("d_fz", fz, F32, ("ev_sm", dumpch),),
                ("d_lg", lg[0], F32, ("ev_sm", dumpch),),
                ("d_tf", t_f[0], F32, ("ev_sm", dumpch),),
                ("d_g1", g1_b, BF16, ("ev_sm", dumpch),),
                ("d_sp", sp, F32, ("fsp", dumpch),),
                ("d_B3", B3, F32, ("dsub", dumpch),),
                ("d_dk", dk, F32, ("dexp", dumpch),),
                ("d_qf", q_f, F32, ("si_qk", dumpch),),
                ("d_kf", k_f, F32, ("si_qk", dumpch),),
                ("d_vb", v_b, BF16, ("si_v", dumpch),),
                ("d_invn", invn, F32, ("rsq", dumpch),),
                ("d_nrm2", nrm2, F32, ("nmax", dumpch),),
                
                ("d_sumx", sumx, F32, ("sumx", dumpch),),
                ("d_sumq", sumq, F32, ("sumq", dumpch),),
                ("d_mean", mean, F32, ("ln1v", dumpch),),
                ("d_var", var, F32, ("ln1v", dumpch),),
                ("d_istd", istd, F32, ("istd1", dumpch),),
                
                ("d_qs", qs_tm, BF16, ("qs", dumpch),),
                ("d_kns", kns_tm, BF16, ("kns", dumpch),),
                ("d_kp", kp_tm, BF16, ("kp", dumpch),),
                ("d_qsfm", qs_fm, BF16, ("ev_qsT", dumpch),),
                ("d_knfm", kn_fm, BF16, ("ev_knT", dumpch),),
                ("d_P", P_bf, BF16, ("maskw", 0, 1)),
                ("d_Sbf", S_bf, BF16, ("T0", dumpch),),
                ("d_qv", qv_tm, BF16, ("qv", dumpch),),

                ("d_S1", S, F32, ("SupdB", dumpch),),
                ("d_gate", gate, F32, ("gate", dumpch),),
                ("d_og", og, F32, ("ev_og", dumpch),),
                ("d_ogn", ogn_tm, BF16, ("ogn", dumpch),),
                ("d_tp", tp_b, BF16, ("tp", dumpch),),
                ("d_e8", e8, F32, ("smax", dumpch),),
                ("d_gfin", gfin, F32, ("tp", dumpch),),
                ("d_nmx", nmx[0], F32, ("ev_sm", dumpch),),
            ]
            for nm, t, dt_, ms in DBG:
                shp = [t[:].shape[0], t[:].shape[1]]
                dd = nc.dram_tensor(nm, shp, dt_, kind="ExternalOutput")
                sch.add("gpsimd", ("dbg", nm), [ms],
                        lambda e, dd=dd, t=t: e.dma_start(out=dd[:], in_=t[:]), 16)

        sch.finalize()

        @block.sync
        def _(eng):
            sch.emit("sync", eng, sems)
            eng.wait_ge(sems["sync"], sch.totals["sync"])

        @block.vector
        def _(eng):
            sch.emit("vector", eng, sems)

        @block.tensor
        def _(eng):
            sch.emit("tensor", eng, sems)

        @block.scalar
        def _(eng):
            sch.emit("scalar", eng, sems)

        @block.gpsimd
        def _(eng):
            sch.emit("gpsimd", eng, sems)
            if sch.totals.get("gpsimd"):
                eng.wait_ge(sems["gpsimd"], sch.totals["gpsimd"])

    return nc


def _build_cons():
    jj = np.arange(128)
    trineg = -np.triu(np.ones((128, 128), dtype=np.float32))
    halfblk = np.zeros((128, 128), np.float32)
    for i in range(128):
        mid = 31 if i < 64 else 95
        halfblk[jj <= mid, i] = -1.0
    allneg = -np.ones((128, 128), dtype=np.float32)
    ident = np.eye(128, dtype=np.float32)
    neg31 = np.zeros((128, 128), np.float32); neg31[0:32, :] = -1.0
    neg95 = np.zeros((128, 128), np.float32); neg95[0:96, :] = -1.0
    m64 = np.zeros((128, 64), np.float32)
    for p in range(128):
        m64[p, p % 64:] = 1.0
    pad = np.zeros((128, 192), np.float32)
    pad[:, 1100 - 1088] = 1e-5
    pad[:, 1101 - 1088] = 1.0
    return np.ascontiguousarray(np.concatenate(
        [trineg - halfblk, halfblk - trineg, allneg - trineg, allneg - neg31,
         neg95 - neg31, trineg, neg31, ident, m64, pad], axis=1))


def _arrange_w(w):
    dout, din = w.shape
    wt = np.ascontiguousarray(w.T)
    return np.ascontiguousarray(
        wt.reshape(din // 128, 128, dout).transpose(1, 0, 2).reshape(128, (din // 128) * dout))


def kernel(L0_attn_x, L0_attn_Wq, L0_attn_Wk, L0_attn_Wv, L0_attn_Wf,
           L0_attn_Wg1, L0_attn_Wg2, L0_attn_ln_w, L0_attn_Wo,
           L0_ffn_weight, L0_ffn_gate_w, L0_ffn_lora_U, L0_ffn_lora_V):
    x = np.ascontiguousarray(np.asarray(L0_attn_x, dtype=np.float32))
    B = x.shape[0]

    wq = _arrange_w(np.asarray(L0_attn_Wq, np.float32))
    wk = _arrange_w(np.asarray(L0_attn_Wk, np.float32))
    wv = _arrange_w(np.asarray(L0_attn_Wv, np.float32))
    wo = _arrange_w(np.asarray(L0_attn_Wo, np.float32))
    wf = _arrange_w(np.asarray(L0_ffn_weight, np.float32))
    smalls = np.concatenate([
        np.asarray(L0_attn_Wf, np.float32),
        np.asarray(L0_ffn_gate_w, np.float32),
        np.asarray(L0_attn_Wg1, np.float32),
        np.asarray(L0_ffn_lora_U, np.float32).reshape(128, 1024),
    ], axis=0)
    sm = _arrange_w(smalls)
    wg2 = np.ascontiguousarray(np.asarray(L0_attn_Wg2, np.float32).T)
    vfl = np.ascontiguousarray(
        np.asarray(L0_ffn_lora_V, np.float32).transpose(0, 2, 1).reshape(128, 1024))
    lnw = np.ascontiguousarray(np.asarray(L0_attn_ln_w, np.float32).reshape(8, 128).T)

    cons = _build_cons()

    nc = build_program()
    shared = {"wq": wq, "wk": wk, "wv": wv, "wo": wo, "wf": wf, "sm": sm,
              "wg2": wg2, "vf": vfl, "lnw": lnw, "cons": cons}
    in_maps = [dict(shared, x=np.ascontiguousarray(x[b])) for b in range(B)]
    res = run_bass_kernel_spmd(nc, in_maps, core_ids=list(range(B)))
    out = np.stack([np.asarray(res.results[b]["y"]) for b in range(B)], axis=0)
    return out.astype(np.float32)


if __name__ == "__main__":
    rng = np.random.default_rng(0)
    print("kernel module loaded")
